# version 03 (final)
# baseline (speedup 1.0000x reference)
import numpy as np
import ml_dtypes
import concourse.bass as bass
import concourse.mybir as mybir
from concourse.bass_utils import run_bass_kernel_spmd

F32 = mybir.dt.float32
BF16 = mybir.dt.bfloat16
AF = mybir.ActivationFunctionType
ALU = mybir.AluOpType
AX = mybir.AxisListType.X
BF = ml_dtypes.bfloat16

D, NT, TOK, NG, H, HD, S = 1024, 8, 2048, 4, 16, 64, 8192
EPS_LN, EPS_RMS = 1e-5, 1e-6
VELT = float(2 ** 20 + 1)


def _perm_interleaved(nheads):
    idx = []
    for p in range(nheads // 4):
        for h in range(4 * p, 4 * p + 4):
            idx += [64 * h + 2 * i for i in range(32)]
        for h in range(4 * p, 4 * p + 4):
            idx += [64 * h + 2 * i + 1 for i in range(32)]
    return np.array(idx)

def _perm_halfsplit(nheads):
    idx = []
    for p in range(nheads // 4):
        for h in range(4 * p, 4 * p + 4):
            idx += [64 * h + i for i in range(32)]
        for h in range(4 * p, 4 * p + 4):
            idx += [64 * h + 32 + i for i in range(32)]
    return np.array(idx)

def _rope_tables(positions):
    out = {}
    for name, base in (("g", 1e6), ("l", 1e4)):
        inv = (1.0 / (np.float32(base) ** (np.arange(32, dtype=np.float32) / np.float32(32)))).astype(np.float32)
        ang = positions.astype(np.float32)[None, :] * inv[:, None]
        ang = np.tile(ang, (4, 1))
        out[name] = (np.cos(ang).astype(BF), np.sin(ang).astype(BF))
    return out

def _gumbel_noise():
    import jax
    with jax.default_device(jax.devices("cpu")[0]):
        g = jax.random.gumbel(jax.random.key(42), (2, S, 8), dtype="float32")
    return (np.asarray(g) * np.float32(0.05)).astype(np.float32)

def host_prep(inputs):
    x = np.ascontiguousarray(inputs["L0_attn_hidden"])
    p16, p4, h16 = _perm_interleaved(16), _perm_interleaved(4), _perm_halfsplit(16)

    def t8(wn):
        return np.ascontiguousarray(wn.reshape(8, 128, -1))

    shared = {
        "w_gq": t8(inputs["L0_attn_g_wq"][:, p16]),
        "w_gkvd": t8(inputs["L0_attn_g_kv_down"]),
        "w_gkup": np.ascontiguousarray(inputs["L0_attn_g_k_up"][:, p4].reshape(2, 128, 256)),
        "w_gvup": np.ascontiguousarray(inputs["L0_attn_g_v_up"].reshape(2, 128, 256)),
        "w_gwo": t8(inputs["L0_attn_g_wo"]),
        "w_lq": t8(inputs["L0_attn_l_wq"][:, p16]),
        "w_lk": t8(inputs["L0_attn_l_wk"][:, p4]),
        "w_lv": t8(inputs["L0_attn_l_wv"]),
        "w_lwo": t8(inputs["L0_attn_l_wo"]),
        "g_n1w": np.ascontiguousarray(inputs["L0_attn_norm1_w"].reshape(8, 128).T),
        "g_n2w": np.ascontiguousarray(inputs["L0_attn_norm2_w"].reshape(8, 128).T),
        "w_fq": t8(inputs["L0_ffn_wq"][:, h16]),
        "w_fk": t8(inputs["L0_ffn_wk"][:, h16]),
        "w_fv": t8(inputs["L0_ffn_wv"]),
        "w_fwo": t8(inputs["L0_ffn_wo"]),
        "f_s1": np.ascontiguousarray(inputs["L0_ffn_scale1"].reshape(8, 128).T),
        "f_s2": np.ascontiguousarray(inputs["L0_ffn_scale2"].reshape(8, 128).T),
        "w_router": np.ascontiguousarray(inputs["L0_ffn_router_w"].reshape(8, 128, 8)),
        "w_m1": np.ascontiguousarray(inputs["L0_ffn_w1"].reshape(8, 8, 128, D)),
        "w_m2": np.ascontiguousarray(inputs["L0_ffn_w2"].reshape(8, 8, 128, D)),
        "w_mo": np.ascontiguousarray(inputs["L0_ffn_wout"].reshape(8, 8, 128, D)),
        "b_m1": np.ascontiguousarray(inputs["L0_ffn_b1"].reshape(8, 8, 128).transpose(2, 0, 1).reshape(128, 64)),
        "b_m2": np.ascontiguousarray(inputs["L0_ffn_b2"].reshape(8, 8, 128).transpose(2, 0, 1).reshape(128, 64)),
        "b_mo": np.ascontiguousarray(inputs["L0_ffn_bout"]),
        "w_aq": t8(inputs["L1_attn_wq"].T[:, p16]),
        "w_ak": t8(inputs["L1_attn_wk"].T[:, p16]),
        "w_av": t8(np.ascontiguousarray(inputs["L1_attn_wv"].T)),
        "w_awo": t8(np.ascontiguousarray(inputs["L1_attn_wo"].T)),
        "w_l1": t8(np.ascontiguousarray(inputs["L1_ffn_weight"].T)),
        "b_l1": np.ascontiguousarray(inputs["L1_ffn_bias"].reshape(8, 128).T),
    }
    noise = _gumbel_noise()

    ones_k1 = np.ones((1, 128), dtype=np.float32).astype(BF)
    ones_c = np.ones((128, 1), dtype=np.float32).astype(BF)
    eye128 = np.eye(128, dtype=np.float32).astype(BF)
    bq = np.zeros((4, 128, 16), dtype=np.float32)
    bcq = np.zeros((4, 16, 128), dtype=np.float32)
    for p in range(4):
        for j in range(4):
            bq[p, 32 * j:32 * j + 32, 4 * p + j] = 1.0
            bcq[p, 4 * p + j, 32 * j:32 * j + 32] = 1.0
    bq = bq.transpose(1, 0, 2).reshape(128, 64).astype(BF)
    bcq = bcq.transpose(1, 0, 2).reshape(16, 512).astype(BF)
    zsel = np.zeros((16, 16, 128), dtype=np.float32)
    for h in range(16):
        zsel[h, h, :] = 1.0
    zsel = zsel.transpose(1, 0, 2).reshape(16, 2048).astype(BF)

    in_maps = []
    for core in range(8):
        b, r = core // 4, core % 4
        toks = np.concatenate([np.arange(2048 * a + 512 * r, 2048 * a + 512 * r + 512)
                               for a in range(NG)])
        xt = np.ascontiguousarray(x[b, toks, :].T.reshape(8, 128, TOK))
        tabs = _rope_tables(toks)
        ncore = np.ascontiguousarray(noise[b, toks, :].reshape(16, 128, 8)
                                     .transpose(1, 0, 2).reshape(128, 128))
        kk = np.arange(128)[:, None]
        qq = np.arange(512)[None, :]
        gm = np.zeros((16, 128, 512), dtype=np.float32)
        for j in range(16):
            gm[j] = (128 * j + kk <= 512 * r + qq)
        lm = np.zeros((18, 128, 512), dtype=np.float32)
        for j in range(18):
            diff = (512 * r + qq) - (128 * j - 256 + kk)
            lm[j] = ((diff >= 0) & (diff < 256))
        m = {"x_in": xt, "noise": ncore,
             "cos_g": tabs["g"][0], "sin_g": tabs["g"][1],
             "cos_l": tabs["l"][0], "sin_l": tabs["l"][1],
             "mask_g": gm.astype(BF), "mask_l": lm.astype(BF),
             "ones_k1": ones_k1, "ones_c": ones_c, "eye128": eye128,
             "bq": bq, "bcq": bcq, "zsel": zsel}
        m.update(shared)
        in_maps.append(m)
    return in_maps



def build(nblocks=4, dump=None):
    nc = bass.Bass("TRN2", target_bir_lowering=False)

    x_in = nc.dram_tensor("x_in", [NT, 128, TOK], F32, kind="ExternalInput").ap()
    out_e = nc.dram_tensor("out", [NT, 128, TOK], F32, kind="ExternalOutput").ap()

    def din(name, shape, dt=F32):
        return nc.dram_tensor(name, shape, dt, kind="ExternalInput").ap()

    noise_i = din("noise", [128, 128])
    tabs_i = {"g": (din("cos_g", [128, TOK], BF16), din("sin_g", [128, TOK], BF16)),
              "l": (din("cos_l", [128, TOK], BF16), din("sin_l", [128, TOK], BF16))}
    mask_g = din("mask_g", [16, 128, 512], BF16)
    mask_l = din("mask_l", [18, 128, 512], BF16)
    ones_i = din("ones_k1", [1, 128], BF16)
    ones_c_i = din("ones_c", [128, 1], BF16)
    eye_i = din("eye128", [128, 128], BF16)
    bq_i = din("bq", [128, 64], BF16)
    bcq_i = din("bcq", [16, 512], BF16)
    zsel_i = din("zsel", [16, 2048], BF16)
    wd = {}
    for nm, shape in [("w_gq", [8, 128, 1024]), ("w_gkvd", [8, 128, 256]),
                      ("w_gkup", [2, 128, 256]), ("w_gvup", [2, 128, 256]),
                      ("w_gwo", [8, 128, 1024]), ("w_lq", [8, 128, 1024]),
                      ("w_lk", [8, 128, 256]), ("w_lv", [8, 128, 256]),
                      ("w_lwo", [8, 128, 1024]), ("w_fq", [8, 128, 1024]),
                      ("w_fk", [8, 128, 1024]), ("w_fv", [8, 128, 1024]),
                      ("w_fwo", [8, 128, 1024]), ("w_router", [8, 128, 8]),
                      ("w_m1", [8, 8, 128, 1024]), ("w_m2", [8, 8, 128, 1024]),
                      ("w_mo", [8, 8, 128, 1024]), ("w_aq", [8, 128, 1024]),
                      ("w_ak", [8, 128, 1024]), ("w_av", [8, 128, 1024]),
                      ("w_awo", [8, 128, 1024]), ("w_l1", [8, 128, 1024])]:
        wd[nm] = din(nm, shape)
    g_n1w, g_n2w = din("g_n1w", [128, 8]), din("g_n2w", [128, 8])
    f_s1, f_s2 = din("f_s1", [128, 8]), din("f_s2", [128, 8])
    b_m1, b_m2 = din("b_m1", [128, 64]), din("b_m2", [128, 64])
    b_mo = din("b_mo", [8, 1024])
    b_l1 = din("b_l1", [128, 8])

    r0 = nc.dram_tensor("r0", [NT, 128, TOK], F32, kind="Internal").ap()
    r1 = nc.dram_tensor("r1", [NT, 128, TOK], F32, kind="Internal").ap()
    K_MHA, V_MHA = 1024 * TOK, TOK * (16 * 65)
    K_GQA, V_GQA = 256 * TOK, TOK * (4 * 65)
    ag = {}
    for nm, sz in (("g", K_GQA + V_GQA), ("l", K_GQA + V_GQA),
                   ("f", K_MHA + V_MHA), ("a", K_MHA + V_MHA)):
        ag[nm] = (nc.dram_tensor(f"agi_{nm}", [1, sz], BF16, kind="Internal").ap(),
                  nc.dram_tensor(f"ago_{nm}", [4, sz], BF16, kind="Internal").ap())
    ar_i = nc.dram_tensor("ar_i", [1, 128], F32, kind="Internal").ap()
    ar_o = nc.dram_tensor("ar_o", [1, 128], F32, kind="Internal").ap()
    wmx_d = nc.dram_tensor("wmx_d", [1, 128], F32, kind="Internal").ap()
    REPL_KV = [[0, 1, 2, 3], [4, 5, 6, 7]]
    REPL_ALL = [[0, 1, 2, 3, 4, 5, 6, 7]]

    from contextlib import ExitStack
    with ExitStack() as _st:
        A_ = _st.enter_context(nc.sbuf_tensor("A", [128, NT * TOK], F32))
        Bb = _st.enter_context(nc.sbuf_tensor("Bb", [128, NT * TOK], BF16))
        Cb = _st.enter_context(nc.sbuf_tensor("Cb", [128, NT * TOK], BF16))
        Db = _st.enter_context(nc.sbuf_tensor("Db", [128, NT * TOK], BF16))
        RCH = _st.enter_context(nc.sbuf_tensor("RCH", [128, 4096], F32))
        PR = _st.enter_context(nc.sbuf_tensor("PR", [128, 4096], BF16))
        SM = _st.enter_context(nc.sbuf_tensor("SM", [128, 4096], F32))
        SB = _st.enter_context(nc.sbuf_tensor("SB", [128, 3904], BF16))
        PSA = _st.enter_context(nc.psum_tensor("PSA", [128, 1024], F32))
        PSB = _st.enter_context(nc.psum_tensor("PSB", [128, 1024], F32))
        PV0 = _st.enter_context(nc.psum_tensor("PV0", [128, 512], F32))
        PV1 = _st.enter_context(nc.psum_tensor("PV1", [128, 512], F32))
        M1 = _st.enter_context(nc.psum_tensor("M1", [128, 512], F32))
        M2 = _st.enter_context(nc.psum_tensor("M2", [128, 512], F32))
        sSY = _st.enter_context(nc.semaphore("sSY"))
        sGD = _st.enter_context(nc.semaphore("sGD"))
        sTE = _st.enter_context(nc.semaphore("sTE"))
        sVE = _st.enter_context(nc.semaphore("sVE"))
        sSE = _st.enter_context(nc.semaphore("sSE"))
        sGC = _st.enter_context(nc.semaphore("sGC"))
        blk = _st.enter_context(nc.Block())
        sems = {"SY": sSY, "GD": sGD, "TE": sTE, "VE": sVE, "SE": sSE, "GC": sGC}
        cnt = {k: 0 for k in sems}

        def inc(ins, s, n=1):
            cnt[s] += n
            ins.then_inc(sems[s], n)
            return cnt[s]

        def wt(e, s, v):
            if v > 0:
                e.wait_ge(sems[s], v)

        def SY(f): blk.sync(f)
        def TE(f): blk.tensor(f)
        def VE(f): blk.vector(f)
        def SE(f): blk.scalar(f)
        def GP(f): blk.gpsimd(f)

        def vdrain(v, i):
            mk = inc(i, "VE")
            v.wait_ge(sems["VE"], mk)
            return mk

        def fence():
            eng_sem = ((SY, "SY"), (TE, "TE"), (VE, "VE"), (SE, "SE"), (GP, "GD"))
            snap = dict(cnt)
            for reg, s_own in eng_sem:
                def f(e, snap=snap, s_own=s_own):
                    for s in ("SY", "GD", "TE", "VE", "SE"):
                        wt(e, s, snap[s])
                    e.sem_inc(sems[s_own], 1)
                reg(f)
                cnt[s_own] += 1
            snap2 = dict(cnt)
            for reg, _ in eng_sem:
                def f(e, snap2=snap2):
                    for s in ("SY", "GD", "TE", "VE", "SE"):
                        wt(e, s, snap2[s])
                reg(f)

        def Av(ft): return A_[:, ft * TOK:(ft + 1) * TOK]
        def Bv(ft): return Bb[:, ft * TOK:(ft + 1) * TOK]
        def Cv(ft): return Cb[:, ft * TOK:(ft + 1) * TOK]
        RBF = RCH[:, :].bitcast(BF16)
        DBF32 = Db[:, :].bitcast(F32)

        SM_noise = SM[:, 3584:3712]
        SM_bm1 = SM[:, 3712:3776]
        SM_bm2 = SM[:, 3776:3840]
        SM_bl1 = SM[:, 3840:3848]
        SM_gain = SM[:, 3848:3856]
        SM_qkr = SM[0:16, 3072:3584]
        SM_Z = SM[0:16, 2048:2560]
        SM_rz = SM[0:16, 2560:3072]
        SM_amax = SM[0:1, 2816:2817]
        SM_wmax = SM[0:1, 2817:2818]
        SM_arx = SM[0:1, 2560:2688]
        SM_arw = SM[0:1, 2688:2816]
        SM_sxpp = SM[:, 2820:2821]
        SM_swpp = SM[:, 2821:2822]
        SM_rspp = SM[:, 2822:2823]
        SM_absx = SM[:, 2824:2825]
        SM_absw = SM[:, 2826:2827]
        SM_wpart = SM[:, 2828:2829]
        SB_ones = SB[0:1, 0:128]
        SB_onesT = SB[:, 64:65]
        SB_eye = SB[:, 128:256]
        def SB_bq(p): return SB[:, 256 + 16 * p:272 + 16 * p]
        def SB_bcq(p): return SB[0:16, 320 + 128 * p:448 + 128 * p]
        SB_zsel = SB[0:16, 832:2880]
        SB_bmo = SB[0:8, 2880:3904]

        def _c(sy):
            inc(sy.dma_start(out=SB[0:1, 0:128], in_=ones_i[:, :]), "SY", 16)
            inc(sy.dma_start(out=SB[:, 64:65], in_=ones_c_i[:, :]), "SY", 16)
            inc(sy.dma_start(out=SB[:, 128:256], in_=eye_i[:, :]), "SY", 16)
            inc(sy.dma_start(out=SB[:, 256:320], in_=bq_i[:, :]), "SY", 16)
            inc(sy.dma_start(out=SB[0:16, 320:832], in_=bcq_i[:, :]), "SY", 16)
            inc(sy.dma_start(out=SB_zsel, in_=zsel_i[:, :]), "SY", 16)
            inc(sy.dma_start(out=SM_noise, in_=noise_i[:, :]), "SY", 16)
            inc(sy.dma_start(out=SM_bm1, in_=b_m1[:, :]), "SY", 16)
            inc(sy.dma_start(out=SM_bm2, in_=b_m2[:, :]), "SY", 16)
            inc(sy.dma_start(out=SM_bl1, in_=b_l1[:, :]), "SY", 16)
        SY(_c)
        def _c2(g):
            inc(g.dma_start(out=SB_bmo, in_=b_mo[:, :]), "GD", 16)
        GP(_c2)
        SM_eps5 = SM[:, 3856:3857]
        SM_eps6 = SM[:, 3857:3858]
        def _c3(v):
            v.memset(SM_eps5, EPS_LN)
            inc(v.memset(SM_eps6, EPS_RMS), "VE")
        VE(_c3)
        nc.const_aps.register(F32, EPS_LN, SM_eps5)
        nc.const_aps.register(F32, EPS_RMS, SM_eps6)
        fence()

        def ph_boundary(src_r, dst_r, add_out, want_a, want_xn):
            ccw = 512
            mo, ro = SM[0:1, 0:512], SM[0:1, 512:1024]
            t2, sq2 = SM[0:1, 1024:1536], SM[0:1, 1536:2048]
            prc = lambda ft: PR[:, (ft % 2) * 512:(ft % 2) * 512 + 512]
            prq = lambda ft: PR[:, 1024 + (ft % 2) * 512:1536 + (ft % 2) * 512]
            scr = PR[:, :].bitcast(F32)[:, 1024:1536]
            rowm = PR[0:1, 3072:3584]
            rowr = PR[0:1, 3584:4096]
            want_x = want_a or want_xn
            assert add_out or want_x
            prev_load = prev_ve = prev_se = 0
            psa_free = psb_free = 0
            t_bc = 0
            tem_slot = {}

            def stats_row(ps):
                tem = cnt["TE"]
                def f(v, tem=tem, ps=ps):
                    wt(v, "TE", tem)
                    v.tensor_scalar(out=mo, in0=ps[0:1, 0:512], scalar1=1.0 / D,
                                    scalar2=None, op0=ALU.mult)
                    inc(v.tensor_scalar(out=t2, in0=ps[0:1, 512:1024], scalar1=1.0 / D,
                                        scalar2=None, op0=ALU.mult), "VE")
                VE(f)
                ps_free = cnt["VE"]
                def f(s, v1=ps_free):
                    wt(s, "VE", v1)
                    inc(s.activation(out=sq2, in_=mo, func=AF.Square), "SE")
                SE(f)
                s1 = cnt["SE"]
                def f(v, s1=s1):
                    wt(v, "SE", s1)
                    inc(v.tensor_sub(out=t2, in0=t2, in1=sq2), "VE")
                VE(f)
                v2 = cnt["VE"]
                def f(s, v2=v2):
                    wt(s, "VE", v2)
                    s.activation(out=ro, in_=t2, func=AF.Ln, bias=EPS_LN, scale=1.0)
                    inc(s.activation(out=ro, in_=ro, func=AF.Exp, scale=-0.5), "SE")
                SE(f)
                s2 = cnt["SE"]
                def f(v, s2=s2, tb=t_bc):
                    wt(v, "SE", s2)
                    wt(v, "TE", tb)
                    v.tensor_copy(out=rowm, in_=mo)
                    inc(v.tensor_copy(out=rowr, in_=ro), "VE")
                VE(f)
                v3 = cnt["VE"]
                def f(t, v3=v3):
                    wt(t, "VE", v3)
                    t.matmul(out=M1[:, :], lhsT=SB_ones, rhs=rowm, start=True, stop=True)
                    inc(t.matmul(out=M2[:, :], lhsT=SB_ones, rhs=rowr,
                                 start=True, stop=True), "TE")
                TE(f)
                return ps_free, cnt["TE"]

            for cc in range(4):
                c0 = cc * ccw
                m = {}
                def f(sy, c0=c0, m=m, pl=prev_load, pv=prev_ve, pse=prev_se):
                    wt(sy, "SY", pl)
                    wt(sy, "VE", pv)
                    wt(sy, "SE", pse)
                    m["r"] = inc(sy.dma_start(
                        out=RCH[:, :].rearrange("p (f c) -> p f c", c=ccw),
                        in_=src_r[:, :, c0:c0 + ccw].rearrange("f p c -> p f c")), "SY", 16)
                SY(f)
                if add_out:
                    for ft in range(NT):
                        mf = {}
                        g = tem_slot.get(ft % 2, 0)
                        def f(v, ft=ft, c0=c0, mf=mf, g=g):
                            wt(v, "TE", g)
                            mf["c"] = inc(v.tensor_copy(out=prc(ft),
                                                        in_=Av(ft)[:, c0:c0 + ccw]), "VE")
                        VE(f)
                        def f(s, ft=ft, c0=c0, mf=mf, g=g):
                            wt(s, "TE", g)
                            mf["q"] = inc(s.activation(out=prq(ft), in_=Av(ft)[:, c0:c0 + ccw],
                                                       func=AF.Square), "SE")
                        SE(f)
                        def f(t, ft=ft, mf=mf, pf=psa_free):
                            if ft == 0:
                                wt(t, "VE", pf)
                            wt(t, "VE", mf["c"])
                            t.matmul(out=PSA[0:1, 0:512], lhsT=SB_onesT, rhs=prc(ft),
                                     start=(ft == 0), stop=(ft == NT - 1))
                            wt(t, "SE", mf["q"])
                            inc(t.matmul(out=PSA[0:1, 512:1024], lhsT=SB_onesT, rhs=prq(ft),
                                         start=(ft == 0), stop=(ft == NT - 1)), "TE")
                            tem_slot[ft % 2] = cnt["TE"]
                        TE(f)
                    psa_free, t_bc = stats_row(PSA)
                for ft in range(NT):
                    mf = {}
                    g = tem_slot.get(ft % 2, 0)
                    def f(v, ft=ft, c0=c0, m=m, mf=mf, g=g, tb=t_bc):
                        if ft == 0:
                            wt(v, "SY", m["r"])
                            if add_out:
                                wt(v, "TE", tb)
                        rsl = RCH[:, ft * ccw:(ft + 1) * ccw]
                        i = None
                        if add_out:
                            v.tensor_sub(out=scr, in0=Av(ft)[:, c0:c0 + ccw], in1=M1[:, :])
                            v.tensor_tensor(out=scr, in0=scr, in1=M2[:, :], op=ALU.mult)
                            i = v.tensor_add(out=rsl, in0=rsl, in1=scr)
                        if want_x:
                            wt(v, "TE", g)
                            i = v.tensor_copy(out=prc(ft), in_=rsl)
                        mf["c"] = inc(i, "VE")
                    VE(f)
                    if want_x:
                        def f(s, ft=ft, mf=mf, g=g):
                            wt(s, "TE", g)
                            wt(s, "VE", mf["c"])
                            mf["q"] = inc(s.activation(out=prq(ft),
                                                       in_=RCH[:, ft * ccw:(ft + 1) * ccw],
                                                       func=AF.Square), "SE")
                        SE(f)
                    def f(sy, ft=ft, c0=c0, mf=mf):
                        wt(sy, "VE", mf["c"])
                        mf["w"] = inc(sy.dma_start(out=dst_r[ft, :, c0:c0 + ccw],
                                                   in_=RCH[:, ft * ccw:(ft + 1) * ccw]), "SY", 16)
                    SY(f)
                    if want_x:
                        def f(t, ft=ft, mf=mf, pf=psb_free):
                            if ft == 0:
                                wt(t, "VE", pf)
                            wt(t, "VE", mf["c"])
                            t.matmul(out=PSB[0:1, 0:512], lhsT=SB_onesT, rhs=prc(ft),
                                     start=(ft == 0), stop=(ft == NT - 1))
                            wt(t, "SE", mf["q"])
                            inc(t.matmul(out=PSB[0:1, 512:1024], lhsT=SB_onesT, rhs=prq(ft),
                                         start=(ft == 0), stop=(ft == NT - 1)), "TE")
                            tem_slot[ft % 2] = cnt["TE"]
                        TE(f)
                prev_load = cnt["SY"]
                prev_se = cnt["SE"]
                if want_x:
                    psb_free, t_bc = stats_row(PSB)
                    def f(v, c0=c0, tb=t_bc):
                        wt(v, "TE", tb)
                        i = None
                        for ft in range(NT):
                            rsl = RCH[:, ft * ccw:(ft + 1) * ccw]
                            v.tensor_sub(out=scr, in0=rsl, in1=M1[:, :])
                            if want_a:
                                i = v.tensor_tensor(out=Av(ft)[:, c0:c0 + ccw], in0=scr,
                                                    in1=M2[:, :], op=ALU.mult)
                                if want_xn:
                                    i = v.tensor_copy(out=Bv(ft)[:, c0:c0 + ccw],
                                                      in_=Av(ft)[:, c0:c0 + ccw])
                            else:
                                i = v.tensor_tensor(out=Bv(ft)[:, c0:c0 + ccw], in0=scr,
                                                    in1=M2[:, :], op=ALU.mult)
                        inc(i, "VE")
                    VE(f)
                prev_ve = cnt["VE"]

        def ph_rms(gains_d):
            m0 = {}
            rr = SM[0:1, 0:512]
            def f(sy, m0=m0):
                m0["g"] = inc(sy.dma_start(out=SM_gain, in_=gains_d[:, :]), "SY", 16)
            SY(f)
            tem_slot = {}
            prev_copy = 0
            prev_bc = 0
            for cc in range(4):
                c0 = cc * 512
                for ft in range(NT):
                    m = {}
                    g = tem_slot.get(ft % 2, 0)
                    def f(s, ft=ft, c0=c0, m=m, g=g):
                        wt(s, "TE", g)
                        m["q"] = inc(s.activation(out=PR[:, (ft % 2) * 512:(ft % 2) * 512 + 512],
                                                  in_=Av(ft)[:, c0:c0 + 512], func=AF.Square), "SE")
                    SE(f)
                    def f(t, ft=ft, m=m, pc=prev_copy):
                        wt(t, "SE", m["q"])
                        if ft == 0:
                            wt(t, "VE", pc)
                        inc(t.matmul(out=M1[0:1, :], lhsT=SB_onesT,
                                     rhs=PR[:, (ft % 2) * 512:(ft % 2) * 512 + 512],
                                     start=(ft == 0), stop=(ft == NT - 1)), "TE")
                        tem_slot[ft % 2] = cnt["TE"]
                    TE(f)
                tem = cnt["TE"]
                def f(s, tem=tem, pc=prev_copy):
                    wt(s, "TE", tem)
                    wt(s, "VE", pc)
                    s.activation(out=rr, in_=M1[0:1, :], func=AF.Ln, bias=EPS_RMS, scale=1.0 / D)
                    inc(s.activation(out=rr, in_=rr, func=AF.Exp, scale=-0.5), "SE")
                SE(f)
                sem = cnt["SE"]
                def f(v, sem=sem, pb=prev_bc):
                    wt(v, "SE", sem)
                    wt(v, "TE", pb)
                    inc(v.tensor_copy(out=PR[0:1, 3072:3584], in_=rr), "VE")
                VE(f)
                prev_copy = cnt["VE"]
                def f(t, vem=prev_copy):
                    wt(t, "VE", vem)
                    inc(t.matmul(out=M2[:, :], lhsT=SB_ones, rhs=PR[0:1, 3072:3584],
                                 start=True, stop=True), "TE")
                TE(f)
                prev_bc = cnt["TE"]
                def f(v, c0=c0, tbc=prev_bc, m0=m0):
                    wt(v, "TE", tbc)
                    wt(v, "SY", m0["g"])
                    i = None
                    for ft in range(NT):
                        i = v.scalar_tensor_tensor(out=Bv(ft)[:, c0:c0 + 512],
                                                   in0=Av(ft)[:, c0:c0 + 512],
                                                   scalar=SM_gain[:, ft:ft + 1], in1=M2[:, :],
                                                   op0=ALU.mult, op1=ALU.mult)
                    inc(i, "VE")
                VE(f)

        def ph_proj(wname, dout, dst_fn=None, rhs_fn=None, nkt=NT, slot=0, add_to=None,
                    add_init=False):
            wsrc = wd[wname]
            nmt = dout // 128
            dv = Db[:, slot * 8192:slot * 8192 + nkt * dout].rearrange("p (k o) -> p k o", o=dout)
            m = {}
            def f(g, m=m):
                m["w"] = inc(g.dma_start(out=dv, in_=wsrc[:, :, :].rearrange("k p o -> p k o")),
                             "GD", 16)
            GP(f)
            for mt in range(nmt):
                for cc in range(4):
                    c0 = cc * 512
                    ps = M1 if (mt * 4 + cc) % 2 == 0 else M2
                    mm = {}
                    def f(t, mt=mt, cc=cc, c0=c0, ps=ps, m=m, mm=mm):
                        if mt == 0 and cc == 0:
                            wt(t, "GD", m["w"])
                        if ("ev", ps is M1) in m:
                            wt(t, "SE" if add_to is None else "VE", m[("ev", ps is M1)])
                        for kt in range(nkt):
                            rhs = rhs_fn(kt, c0) if rhs_fn else Bv(kt)[:, c0:c0 + 512]
                            i = t.matmul(out=ps[:, :], lhsT=dv[:, kt, 128 * mt:128 * mt + 128],
                                         rhs=rhs, start=(kt == 0), stop=(kt == nkt - 1))
                        mm["m"] = inc(i, "TE")
                    TE(f)
                    if add_to is None:
                        def f(s, mt=mt, c0=c0, ps=ps, m=m, mm=mm):
                            wt(s, "TE", mm["m"])
                            m[("ev", ps is M1)] = inc(
                                s.activation(out=dst_fn(mt)[:, c0:c0 + 512], in_=ps[:, :],
                                             func=AF.Copy), "SE")
                        SE(f)
                    else:
                        def f(v, mt=mt, c0=c0, ps=ps, m=m, mm=mm):
                            wt(v, "TE", mm["m"])
                            dst = add_to(mt)[:, c0:c0 + 512]
                            if add_init:
                                i = v.tensor_copy(out=dst, in_=ps[:, :])
                            else:
                                i = v.tensor_add(out=dst, in0=dst, in1=ps[:, :])
                            m[("ev", ps is M1)] = inc(i, "VE")
                        VE(f)
            return m

        def ph_rope(pairs, tab):
            cosd, sind = tabs_i[tab]
            m = {}
            def f(g, m=m):
                inc(g.dma_start(out=RBF[:, 0:TOK], in_=cosd[:, :]), "GD", 16)
                m["t"] = inc(g.dma_start(out=RBF[:, TOK:2 * TOK], in_=sind[:, :]), "GD", 16)
            GP(f)
            for (ia, ib) in pairs:
                def f(v, ia=ia, ib=ib, m=m):
                    wt(v, "GD", m["t"])
                    At, Bt = Cv(ia), Cv(ib)
                    cosb, sinb = RBF[:, 0:TOK], RBF[:, TOK:2 * TOK]
                    p0, p1 = PR[:, 0:2048], PR[:, 2048:4096]
                    v.tensor_tensor(out=p0, in0=At, in1=cosb, op=ALU.mult)
                    v.tensor_tensor(out=p1, in0=At, in1=sinb, op=ALU.mult)
                    v.tensor_tensor(out=At, in0=Bt, in1=sinb, op=ALU.mult)
                    v.tensor_sub(out=At, in0=p0, in1=At)
                    v.tensor_tensor(out=Bt, in0=Bt, in1=cosb, op=ALU.mult)
                    inc(v.tensor_add(out=Bt, in0=Bt, in1=p1), "VE")
                VE(f)
            return cnt["VE"]

        def ph_qknorm(pairs):
            nh = 4 * len(pairs)
            tem_slot = {}
            prev_copy = 0
            prev_mult = 0
            prev_bcast = 0
            for cc in range(4):
                c0 = cc * 512
                for pi, (ia, ib) in enumerate(pairs):
                    for si, til in enumerate((ia, ib)):
                        m = {}
                        sl = (2 * pi + si) % 2
                        g = tem_slot.get(sl, 0)
                        def f(s, til=til, c0=c0, sl=sl, m=m, g=g):
                            wt(s, "TE", g)
                            m["q"] = inc(s.activation(out=PR[:, sl * 512:sl * 512 + 512],
                                                      in_=Cv(til)[:, c0:c0 + 512],
                                                      func=AF.Square), "SE")
                        SE(f)
                        first = (pi == 0 and si == 0)
                        last = (pi == len(pairs) - 1 and si == 1)
                        def f(t, pi=pi, sl=sl, m=m, first=first, last=last, nh=nh,
                              pc=prev_copy):
                            wt(t, "SE", m["q"])
                            if first:
                                wt(t, "VE", pc)
                            inc(t.matmul(out=M1[0:nh, :], lhsT=SB_bq(pi)[:, 0:nh],
                                         rhs=PR[:, sl * 512:sl * 512 + 512],
                                         start=first, stop=last), "TE")
                            tem_slot[sl] = cnt["TE"]
                        TE(f)
                tem = cnt["TE"]
                def f(s, tem=tem, nh=nh, pc=prev_copy):
                    wt(s, "TE", tem)
                    wt(s, "VE", pc)
                    s.activation(out=SM_qkr[0:nh, :], in_=M1[0:nh, :],
                                 func=AF.Ln, bias=EPS_RMS, scale=1.0 / HD)
                    inc(s.activation(out=SM_qkr[0:nh, :], in_=SM_qkr[0:nh, :],
                                     func=AF.Exp, scale=-0.5), "SE")
                SE(f)
                sem = cnt["SE"]
                def f(v, sem=sem, nh=nh, pb=prev_bcast):
                    wt(v, "SE", sem)
                    wt(v, "TE", pb)
                    inc(v.tensor_copy(out=PR[0:nh, 3072:3584], in_=SM_qkr[0:nh, :]), "VE")
                VE(f)
                prev_copy = cnt["VE"]
                for pi, (ia, ib) in enumerate(pairs):
                    m2 = {}
                    def f(t, pi=pi, vem=prev_copy, m2=m2, nh=nh, pm=prev_mult):
                        wt(t, "VE", vem)
                        wt(t, "VE", pm)
                        m2["b"] = inc(t.matmul(out=M2[:, :], lhsT=SB_bcq(pi)[0:nh, :],
                                               rhs=PR[0:nh, 3072:3584], start=True, stop=True),
                                      "TE")
                    TE(f)
                    def f(v, ia=ia, ib=ib, c0=c0, m2=m2):
                        wt(v, "TE", m2["b"])
                        v.tensor_tensor(out=Cv(ia)[:, c0:c0 + 512], in0=Cv(ia)[:, c0:c0 + 512],
                                        in1=M2[:, :], op=ALU.mult)
                        inc(v.tensor_tensor(out=Cv(ib)[:, c0:c0 + 512], in0=Cv(ib)[:, c0:c0 + 512],
                                            in1=M2[:, :], op=ALU.mult), "VE")
                    VE(f)
                    prev_mult = cnt["VE"]
                prev_bcast = cnt["TE"]

        def ph_stage_k(pairs, agbuf, ve_after):
            kview = agbuf[0:1, :].rearrange("a (r c) -> a r c", c=TOK)
            def f(sy, ve_after=ve_after):
                wt(sy, "VE", ve_after)
                for pi, (ia, ib) in enumerate(pairs):
                    for j in range(4):
                        h = 4 * pi + j
                        inc(sy.dma_start(out=kview[0, 64 * h:64 * h + 32, :],
                                         in_=Cv(ia)[32 * j:32 * j + 32, :]), "SY", 16)
                        inc(sy.dma_start(out=kview[0, 64 * h + 32:64 * h + 64, :],
                                         in_=Cv(ib)[32 * j:32 * j + 32, :]), "SY", 16)
            SY(f)
            return cnt["SY"]

        def ph_v(wname, nkv, agbuf, koff, rhs_fn=None, nkt=NT, slot=0):
            wsrc = wd[wname]
            dout = nkv * 64
            VROW = nkv * 65
            dv = Db[:, slot * 8192:slot * 8192 + nkt * dout].rearrange("p (k o) -> p k o", o=dout)
            def stg(sl):
                return PR[:, sl * VROW:(sl + 1) * VROW].rearrange("p (h c) -> p h c", c=65)
            vview = agbuf[0:1, koff:koff + TOK * VROW].rearrange("a (r c) -> a r c", c=VROW)
            m = {}
            def f(g, m=m):
                m["w"] = inc(g.dma_start(out=dv, in_=wsrc[:, :, :].rearrange("k p o -> p k o")),
                             "GD", 16)
            GP(f)
            def f(v, m=m):
                for sl in range(2):
                    i = v.memset(stg(sl)[:, :, 64:65], 1.0)
                m["o"] = inc(i, "VE")
            VE(f)
            for tt in range(16):
                sl = tt % 2
                mm = {}
                for oc in range((dout + 511) // 512):
                    ow = min(512, dout - oc * 512)
                    ps = M1 if (tt * 2 + oc) % 2 == 0 else M2
                    def f(t, tt=tt, oc=oc, ow=ow, ps=ps, m=m, mm=mm):
                        if tt == 0 and oc == 0:
                            wt(t, "GD", m["w"])
                            wt(t, "VE", m["o"])
                        if ("ev", ps is M1) in m:
                            wt(t, "SE", m[("ev", ps is M1)])
                        for kt in range(nkt):
                            lhsT = rhs_fn(kt, tt) if rhs_fn else Bv(kt)[:, 128 * tt:128 * tt + 128]
                            i = t.matmul(out=ps[:, 0:ow], lhsT=lhsT,
                                         rhs=dv[:, kt, oc * 512:oc * 512 + ow],
                                         start=(kt == 0), stop=(kt == nkt - 1))
                        mm[("m", oc)] = inc(i, "TE")
                    TE(f)
                    def f(s, oc=oc, ow=ow, ps=ps, sl=sl, m=m, mm=mm):
                        wt(s, "TE", mm[("m", oc)])
                        if ("st", sl) in m:
                            wt(s, "SY", m[("st", sl)])
                        m[("ev", ps is M1)] = inc(
                            s.activation(out=stg(sl)[:, oc * 8:oc * 8 + ow // 64, 0:64],
                                         in_=ps[:, 0:ow].rearrange("p (h c) -> p h c", c=64),
                                         func=AF.Copy), "SE")
                    SE(f)
                se_tt = cnt["SE"]
                def f(sy, tt=tt, sl=sl, m=m, se_tt=se_tt):
                    wt(sy, "SE", se_tt)
                    m[("st", sl)] = inc(sy.dma_start(out=vview[0, 128 * tt:128 * tt + 128, :],
                                                     in_=PR[:, sl * VROW:(sl + 1) * VROW]), "SY", 16)
                SY(f)
            return cnt["SY"]

        def ph_ag(name, sy_dep):
            agin, agout = ag[name]
            def f(g, sy_dep=sy_dep):
                wt(g, "SY", sy_dep)
                inc(g.collective_compute("AllGather", ALU.bypass, replica_groups=REPL_KV,
                                         ins=[agin[:, :]], outs=[agout[:, :]]), "GC")
            GP(f)
            return cnt["GC"]

        def ph_qrearr(ve_after):
            def f(sy, ve_after=ve_after):
                wt(sy, "VE", ve_after)
                for m_ in range(8):
                    p, s_ = m_ // 2, m_ % 2
                    ia, ib = 2 * p, 2 * p + 1
                    inc(sy.dma_start(out=Bv(m_)[0:32, :], in_=Cv(ia)[64 * s_:64 * s_ + 32, :]), "SY", 16)
                    inc(sy.dma_start(out=Bv(m_)[32:64, :], in_=Cv(ib)[64 * s_:64 * s_ + 32, :]), "SY", 16)
                    inc(sy.dma_start(out=Bv(m_)[64:96, :], in_=Cv(ia)[64 * s_ + 32:64 * s_ + 64, :]), "SY", 16)
                    inc(sy.dma_start(out=Bv(m_)[96:128, :], in_=Cv(ib)[64 * s_ + 32:64 * s_ + 64, :]), "SY", 16)
            SY(f)
            return cnt["SY"]

        def ph_attention(agname, mha, local, gc_dep):
            agout = ag[agname][1]
            nkv = 16 if mha else 4
            KOFF = (1024 if mha else 256) * TOK
            VROW = nkv * 65
            KP = 16 if mha else 8
            kpsz, vsz, msz = KP * 256, 2 * VROW, 2 * 512
            csz = kpsz + vsz + msz
            msrc = mask_l if local else mask_g
            def kbuf(sl):
                return Cb[:, sl * csz:sl * csz + kpsz].rearrange("p (h c) -> p h c", c=256)
            def vbuf(sl):
                return Cb[:, sl * csz + kpsz:sl * csz + kpsz + vsz].rearrange(
                    "p (k c) -> p k c", c=VROW)
            def mbuf(sl):
                return Cb[:, sl * csz + kpsz + vsz:sl * csz + csz].rearrange(
                    "p (j c) -> p j c", c=512)
            def pva(h):
                if h < 8:
                    return RCH[0:65, 512 * h:512 * h + 512]
                return DBF32[0:65, 4096 + 512 * (h - 8):4096 + 512 * (h - 8) + 512]
            m0 = {}
            def f(v, m0=m0):
                for sl in range(2):
                    kb = kbuf(sl)
                    v.memset(kb[64:128, 0::2, :], 0.0)
                    i = v.memset(kb[0:64, 1::2, :], 0.0)
                m0["pad"] = inc(i, "VE")
            VE(f)
            kq_of = (lambda h: h) if mha else (lambda h: 2 * (h // 4) + (h % 2))
            vc_of = (lambda h: 65 * h) if mha else (lambda h: 65 * (h // 4))

            nch = 0
            use_mark = {}
            last_exp = {}
            last_pve = {}
            last_sh = {}
            last_zsel = 0
            grp_sh = 0
            fin_marks = []
            for a in range(NG):
                t0 = max(0, 16 * a - 2) if local else 0
                t1 = 16 * a + 16
                tiles = list(range(t0, t1))
                chunks = [tiles[i:i + 2] for i in range(0, len(tiles), 2)]
                q0 = a * 512
                for ci, cts in enumerate(chunks):
                    sl = nch % 2
                    nch += 1
                    prev_use = use_mark.get(sl, 0)
                    mk = {}
                    def f(sy, cts=cts, sl=sl, prev_use=prev_use, mk=mk, m0=m0, a=a):
                        wt(sy, "VE", m0["pad"])
                        wt(sy, "GC", gc_dep)
                        if prev_use:
                            wt(sy, "TE", prev_use)
                        kt_lo, n_kt = cts[0], len(cts)
                        j = kt_lo // 4
                        rnk, grp, sub = j % 4, j // 4, kt_lo % 4
                        kreg = agout[rnk, 0:KOFF].rearrange("(d c) -> d c", c=TOK)
                        col0 = 512 * grp + 128 * sub
                        nc_ = 128 * n_kt
                        kb = kbuf(sl)
                        if mha:
                            kv_ = kreg[:, col0:col0 + nc_].rearrange(
                                "(h e d) c -> e d h c", e=2, d=64)
                            inc(sy.dma_start(out=kb[0:64, 0::2, 0:nc_], in_=kv_[0]), "SY", 16)
                            inc(sy.dma_start(out=kb[64:128, 1::2, 0:nc_], in_=kv_[1]), "SY", 16)
                        else:
                            kv_ = kreg[:, col0:col0 + nc_].rearrange("(g d) c -> d g c", d=64)
                            inc(sy.dma_start(out=kb[0:64, 0::2, 0:nc_], in_=kv_), "SY", 16)
                            inc(sy.dma_start(out=kb[64:128, 1::2, 0:nc_], in_=kv_), "SY", 16)
                        vreg = agout[rnk, KOFF:KOFF + TOK * VROW].rearrange("(r c) -> r c", c=VROW)
                        i = inc(sy.dma_start(out=vbuf(sl)[:, 0:n_kt, :],
                                             in_=vreg[col0:col0 + nc_, :].rearrange(
                                                 "(k p) c -> p k c", p=128)), "SY", 16)
                        if local:
                            js = [kt - (16 * a - 2) for kt in cts]
                        else:
                            js = [kt - 16 * a for kt in cts if kt >= 16 * a]
                        for jj, jv in enumerate(js):
                            i = inc(sy.dma_start(out=mbuf(sl)[:, jj, :], in_=msrc[jv, :, :]),
                                    "SY", 16)
                        mk["ld"] = i
                    SY(f)
                    for h in range(H):
                        sps = PSA if h % 2 == 0 else PSB
                        pvp = PV0 if h % 2 == 0 else PV1
                        prb = PR[:, (h % 2) * 2048:(h % 2) * 2048 + 2048]
                        mh = {}
                        def f(t, h=h, cts=cts, sl=sl, sps=sps, mk=mk, mh=mh, q0=q0):
                            wt(t, "SY", mk["ld"])
                            pe = mk.get(("exp", h - 2), last_exp.get(h % 2, 0))
                            wt(t, "SE", pe)
                            for ki in range(len(cts)):
                                i = t.matmul(out=sps[:, 512 * ki:512 * ki + 512],
                                             lhsT=kbuf(sl)[:, kq_of(h), :].rearrange(
                                                 "p (u d) -> p u d", d=128)[:, ki, :],
                                             rhs=Bv(h // 2)[:, q0:q0 + 512],
                                             start=True, stop=True)
                            mh["s"] = inc(i, "TE")
                        TE(f)
                        def f(s, cts=cts, sps=sps, prb=prb, mh=mh, mk=mk, h=h,
                              ci=ci, gsh=grp_sh):
                            wt(s, "TE", mh["s"])
                            if h == 1 and ci == 0:
                                wt(s, "SY", gsh)
                            i = s.activation(out=prb[:, 0:512 * len(cts)],
                                             in_=sps[:, 0:512 * len(cts)], func=AF.Exp, scale=0.125)
                            mk[("exp", h)] = mh["e"] = inc(i, "SE")
                            last_exp[h % 2] = mh["e"]
                        SE(f)
                        if local:
                            js = list(range(len(cts)))
                        else:
                            js = [ki for ki, kt in enumerate(cts) if kt >= 16 * a]
                        if js:
                            def f(v, js=js, sl=sl, prb=prb, mh=mh):
                                wt(v, "SE", mh["e"])
                                for jj, ki in enumerate(js):
                                    i = v.tensor_tensor(out=prb[:, 512 * ki:512 * ki + 512],
                                                        in0=prb[:, 512 * ki:512 * ki + 512],
                                                        in1=mbuf(sl)[:, jj, :], op=ALU.mult)
                                mh["mm"] = inc(i, "VE")
                            VE(f)
                        def f(t, h=h, cts=cts, sl=sl, pvp=pvp, prb=prb, mh=mh, mk=mk, js=js):
                            if js:
                                wt(t, "VE", mh["mm"])
                            else:
                                wt(t, "SE", mh["e"])
                            pv = mk.get(("pve", h - 2), last_pve.get(h % 2, 0))
                            wt(t, "VE", pv)
                            for ki in range(len(cts)):
                                i = t.matmul(out=pvp[0:65, :],
                                             lhsT=vbuf(sl)[:, ki, vc_of(h):vc_of(h) + 65],
                                             rhs=prb[:, 512 * ki:512 * ki + 512],
                                             start=(ki == 0), stop=(ki == len(cts) - 1))
                            mh["p"] = inc(i, "TE")
                            use_mark[sl] = cnt["TE"]
                        TE(f)
                        def f(v, h=h, ci=ci, pvp=pvp, mh=mh, mk=mk):
                            wt(v, "TE", mh["p"])
                            dst = pva(h)
                            if ci == 0:
                                i = v.tensor_copy(out=dst, in_=pvp[0:65, :])
                            else:
                                i = v.tensor_add(out=dst, in0=dst, in1=pvp[0:65, :])
                            mk[("pve", h)] = inc(i, "VE")
                            last_pve[h % 2] = mk[("pve", h)]
                        VE(f)
                ve_all = cnt["VE"]
                mf = {}
                def f(sy, ve_all=ve_all, mf=mf):
                    wt(sy, "VE", ve_all)
                    for h in range(8):
                        inc(sy.dma_start(out=SM_Z[h:h + 1, 0:512],
                                         in_=RCH[64:65, 512 * h:512 * h + 512]), "SY", 16)
                    for h in range(8):
                        i = inc(sy.dma_start(out=SM_Z[8 + h:9 + h, 0:512],
                                             in_=DBF32[64:65, 4096 + 512 * h:4608 + 512 * h]),
                                "SY", 16)
                    mf["z"] = i
                SY(f)
                def f(s, mf=mf):
                    wt(s, "SY", mf["z"])
                    s.activation(out=SM_rz, in_=SM_Z, func=AF.Ln)
                    mf["ln"] = inc(s.activation(out=SM_rz, in_=SM_rz, func=AF.Exp,
                                                scale=-1.0), "SE")
                SE(f)
                def f(v, mf=mf, lz=last_zsel):
                    wt(v, "SE", mf["ln"])
                    wt(v, "TE", lz)
                    mf["rz"] = inc(v.tensor_copy(out=PR[0:16, 3072:3584], in_=SM_rz), "VE")
                VE(f)
                for h in range(H):
                    ps = M1 if h % 2 == 0 else M2
                    mh2 = {}
                    def f(t, h=h, ps=ps, mf=mf, mh2=mh2):
                        wt(t, "VE", mf["rz"])
                        if ("bz", h - 2) in mf:
                            wt(t, "VE", mf[("bz", h - 2)])
                        mh2["b"] = inc(t.matmul(out=ps[:, :],
                                                lhsT=SB_zsel[:, 128 * h:128 * h + 128],
                                                rhs=PR[0:16, 3072:3584], start=True, stop=True), "TE")
                    TE(f)
                    if h % 2 == 0:
                        def f(v, h=h, a=a, ps=ps, mf=mf, mh2=mh2):
                            wt(v, "TE", mh2["b"])
                            mf[("bz", h)] = inc(v.tensor_tensor(
                                out=Bv(h // 2)[0:64, a * 512:a * 512 + 512],
                                in0=pva(h)[0:64, :], in1=ps[0:64, :], op=ALU.mult), "VE")
                        VE(f)
                    else:
                        shsl = (h // 2) % 2
                        def f(v, h=h, ps=ps, mf=mf, mh2=mh2, shsl=shsl,
                              psh=last_sh.get((h // 2) % 2, 0)):
                            wt(v, "TE", mh2["b"])
                            wt(v, "SY", psh)
                            mf[("bz", h)] = inc(v.tensor_tensor(
                                out=PR[0:64, 2048 + shsl * 512:2048 + shsl * 512 + 512],
                                in0=pva(h)[0:64, :], in1=ps[0:64, :], op=ALU.mult), "VE")
                        VE(f)
                        def f(sy, h=h, a=a, mf=mf, shsl=shsl):
                            wt(sy, "VE", mf[("bz", h)])
                            mf[("sh", h)] = inc(sy.dma_start(
                                out=Bv(h // 2)[64:128, a * 512:a * 512 + 512],
                                in_=PR[0:64, 2048 + shsl * 512:2048 + shsl * 512 + 512]), "SY", 16)
                        SY(f)
                        last_sh[shsl] = mf[("sh", h)]
                last_zsel = cnt["TE"]
                grp_sh = cnt["SY"]
                fin_marks.append((cnt["SY"], cnt["VE"]))
            return fin_marks[-1]

        def attn_layer(kind):
            if kind == "g":
                fence(); ph_proj("w_gkvd", 256, dst_fn=lambda mt: Cv(6 + mt), slot=0)
                fence(); ph_proj("w_gkup", 256, dst_fn=lambda mt: Cv(mt),
                        rhs_fn=lambda kt, c0: Cv(6 + kt)[:, c0:c0 + 512], nkt=2, slot=1)
                fence(); ph_rope([(0, 1)], "g")
                fence(); ph_qknorm([(0, 1)])
                maybe_dump("gk", "C")
                fence(); ph_stage_k([(0, 1)], ag["g"][0], cnt["VE"])
                fence(); ph_v("w_gvup", 4, ag["g"][0], K_GQA,
                           rhs_fn=lambda kt, tt: Cv(6 + kt)[:, 128 * tt:128 * tt + 128],
                           nkt=2, slot=1)
                gc = ph_ag("g", cnt["SY"])
                fence(); ph_proj("w_gq", 1024, dst_fn=lambda mt: Cv(mt), slot=0)
                fence(); ph_rope([(0, 1), (2, 3), (4, 5), (6, 7)], "g")
                fence(); ph_qknorm([(0, 1), (2, 3), (4, 5), (6, 7)])
                maybe_dump("gq", "C")
                fence(); ph_qrearr(cnt["VE"])
                fence(); ph_attention("g", mha=False, local=False, gc_dep=gc)
                maybe_dump("gattn", "B")
                fence(); ph_proj("w_gwo", 1024, slot=0, add_to=lambda mt: Av(mt))
            elif kind == "l":
                fence(); ph_proj("w_lk", 256, dst_fn=lambda mt: Cv(mt), slot=0)
                fence(); ph_rope([(0, 1)], "l")
                fence(); ph_qknorm([(0, 1)])
                fence(); ph_stage_k([(0, 1)], ag["l"][0], cnt["VE"])
                fence(); ph_v("w_lv", 4, ag["l"][0], K_GQA, slot=1)
                gc = ph_ag("l", cnt["SY"])
                fence(); ph_proj("w_lq", 1024, dst_fn=lambda mt: Cv(mt), slot=0)
                fence(); ph_rope([(0, 1), (2, 3), (4, 5), (6, 7)], "l")
                fence(); ph_qknorm([(0, 1), (2, 3), (4, 5), (6, 7)])
                fence(); ph_qrearr(cnt["VE"])
                fence(); ph_attention("l", mha=False, local=True, gc_dep=gc)
                fence(); ph_proj("w_lwo", 1024, slot=0, add_to=lambda mt: Av(mt))
            else:
                wk = "w_fk" if kind == "f" else "w_ak"
                wv = "w_fv" if kind == "f" else "w_av"
                wq = "w_fq" if kind == "f" else "w_aq"
                wo = "w_fwo" if kind == "f" else "w_awo"
                fence(); ph_proj(wk, 1024, dst_fn=lambda mt: Cv(mt), slot=0)
                fence(); ph_rope([(0, 1), (2, 3), (4, 5), (6, 7)], "l")
                fence(); ph_stage_k([(0, 1), (2, 3), (4, 5), (6, 7)], ag[kind][0], cnt["VE"])
                fence(); ph_v(wv, 16, ag[kind][0], K_MHA, slot=1)
                gc = ph_ag(kind, cnt["SY"])
                fence(); ph_proj(wq, 1024, dst_fn=lambda mt: Cv(mt), slot=0)
                fence(); ph_rope([(0, 1), (2, 3), (4, 5), (6, 7)], "l")
                fence(); ph_qrearr(cnt["VE"])
                fence(); ph_attention(kind, mha=True, local=False, gc_dep=gc)
                fence(); ph_proj(wo, 1024, slot=0, add_to=lambda mt: Av(mt),
                        add_init=(kind == "a"))

        def ph_moe():
            RT = RCH
            m = {}
            def f(g, m=m):
                m["w"] = inc(g.dma_start(out=Db[:, 0:64].rearrange("p (k o) -> p k o", o=8),
                                         in_=wd["w_router"][:, :, :].rearrange("k p o -> p k o")),
                             "GD", 16)
            GP(f)
            for tt in range(16):
                mm = {}
                def f(t, tt=tt, m=m, mm=mm):
                    if tt == 0:
                        wt(t, "GD", m["w"])
                    if tt >= 2:
                        wt(t, "VE", m[("ev", tt - 2)])
                    ps = M1 if tt % 2 == 0 else M2
                    for kt in range(NT):
                        i = t.matmul(out=ps[:, 0:8],
                                     lhsT=Bv(kt)[:, 128 * tt:128 * tt + 128],
                                     rhs=Db[:, 0:64].rearrange("p (k o) -> p k o", o=8)[:, kt, :],
                                     start=(kt == 0), stop=(kt == NT - 1))
                    mm["m"] = inc(i, "TE")
                TE(f)
                def f(v, tt=tt, m=m, mm=mm):
                    wt(v, "TE", mm["m"])
                    ps = M1 if tt % 2 == 0 else M2
                    m[("ev", tt)] = inc(v.tensor_add(out=RT[:, 8 * tt:8 * tt + 8], in0=ps[:, 0:8],
                                                     in1=SM_noise[:, 8 * tt:8 * tt + 8]), "VE")
                VE(f)
            for tt in range(16):
                lg = RT[:, 8 * tt:8 * tt + 8]
                nm = RT[:, 128 + tt:129 + tt]
                pr_ = RT[:, 256 + 8 * tt:256 + 8 * tt + 8]
                zz = RT[:, 384 + tt:385 + tt]
                t1_ = RT[:, 512 + 8 * tt:512 + 8 * tt + 8]
                mx1 = RT[:, 640 + tt:641 + tt]
                mx2 = RT[:, 768 + tt:769 + tt]
                gmc = RT[:, 896 + 8 * tt:896 + 8 * tt + 8]
                mm = {}
                def f(v, lg=lg, nm=nm, mm=mm, m=m, tt=tt):
                    wt(v, "VE", m[("ev", tt)])
                    mm["n"] = inc(v.reduce_max(out=nm, in_=lg, axis=AX, negate=True), "VE")
                VE(f)
                def f(s, lg=lg, nm=nm, pr_=pr_, mm=mm):
                    wt(s, "VE", mm["n"])
                    mm["e"] = inc(s.activation(out=pr_, in_=lg, func=AF.Exp, bias=nm[:, 0:1]), "SE")
                SE(f)
                def f(v, pr_=pr_, zz=zz, mm=mm):
                    wt(v, "SE", mm["e"])
                    mm["z"] = inc(v.reduce_sum(out=zz, in_=pr_, axis=AX), "VE")
                VE(f)
                def f(s, zz=zz, mm=mm):
                    wt(s, "VE", mm["z"])
                    s.activation(out=zz, in_=zz, func=AF.Ln)
                    mm["rz"] = inc(s.activation(out=zz, in_=zz, func=AF.Exp, scale=-1.0), "SE")
                SE(f)
                def f(v, pr_=pr_, zz=zz, t1_=t1_, mx1=mx1, mx2=mx2, gmc=gmc, mm=mm, tt=tt):
                    wt(v, "SE", mm["rz"])
                    vdrain(v, v.tensor_scalar(out=pr_, in0=pr_, scalar1=zz[:, 0:1],
                                              scalar2=None, op0=ALU.mult))
                    vdrain(v, v.reduce_max(out=mx1, in_=pr_, axis=AX))
                    vdrain(v, v.tensor_scalar(out=t1_, in0=pr_, scalar1=mx1[:, 0:1],
                                              scalar2=None, op0=ALU.is_lt))
                    vdrain(v, v.tensor_tensor(out=t1_, in0=t1_, in1=pr_, op=ALU.mult))
                    vdrain(v, v.reduce_max(out=mx2, in_=t1_, axis=AX))
                    vdrain(v, v.tensor_scalar(out=t1_, in0=pr_, scalar1=mx2[:, 0:1],
                                              scalar2=None, op0=ALU.is_ge))
                    vdrain(v, v.tensor_tensor(out=gmc, in0=t1_, in1=pr_, op=ALU.mult))
                    mm["g"] = inc(v.tensor_copy(out=PR[:, 2048 + 8 * tt:2048 + 8 * tt + 8], in_=gmc),
                                  "VE")
                VE(f)
                def f(t, tt=tt, mm=mm, m=m):
                    wt(t, "VE", mm["g"])
                    ps = M1 if tt % 2 == 0 else M2
                    if tt >= 2:
                        wt(t, "SE", m[("gt", tt - 2)])
                    mm["t"] = inc(t.matmul(out=ps[0:8, 0:128],
                                           lhsT=PR[:, 2048 + 8 * tt:2048 + 8 * tt + 8],
                                           rhs=SB_eye, start=True, stop=True), "TE")
                TE(f)
                def f(s, tt=tt, mm=mm, m=m):
                    wt(s, "TE", mm["t"])
                    ps = M1 if tt % 2 == 0 else M2
                    m[("gt", tt)] = inc(s.activation(out=PR[0:8, 128 * tt:128 * tt + 128],
                                                     in_=ps[0:8, 0:128], func=AF.Copy), "SE")
                SE(f)
            se_gm = cnt["SE"]
            gmT = PR[0:8, 0:2048]
            if dump == "moe_rt":
                def f(sy):
                    wt(sy, "VE", cnt["VE"])
                    wt(sy, "SE", cnt["SE"])
                    inc(sy.dma_start(out=out_e[0, :, 0:1024], in_=RCH[:, 0:1024]), "SY", 16)
                SY(f)
                raise _Done
            if dump == "moe_gm":
                def f(g, se_gm=se_gm):
                    wt(g, "SE", se_gm)
                    inc(g.dma_start(out=out_e[0, 0:8, :], in_=PR[0:8, 0:2048]), "GD", 16)
                GP(f)
                raise _Done
            for e in range(8):
                fence()
                me = {}
                def f(g, e=e, me=me):
                    me["w1"] = inc(g.dma_start(
                        out=Db[:, 0:8192].rearrange("p (k o) -> p k o", o=1024),
                        in_=wd["w_m1"][e].rearrange("k p o -> p k o")), "GD", 16)
                    me["w2"] = inc(g.dma_start(
                        out=Db[:, 8192:16384].rearrange("p (k o) -> p k o", o=1024),
                        in_=wd["w_m2"][e].rearrange("k p o -> p k o")), "GD", 16)
                GP(f)
                def f(sy, e=e, me=me):
                    wt(sy, "SE", se_gm)
                    me["gr"] = inc(sy.dma_start(out=RBF[0:1, 6144:8192], in_=gmT[e:e + 1, :]),
                                   "SY", 16)
                SY(f)
                for cc in range(4):
                    def f(t, cc=cc, me=me):
                        wt(t, "SY", me["gr"])
                        if ("gb", cc - 1) in me:
                            wt(t, "VE", me[("gb", cc - 1)])
                        me[("bm", cc)] = inc(t.matmul(out=M1[:, :], lhsT=SB_ones,
                                                      rhs=RBF[0:1, 6144 + 512 * cc:6656 + 512 * cc],
                                                      start=True, stop=True), "TE")
                    TE(f)
                    def f(v, cc=cc, me=me):
                        wt(v, "TE", me[("bm", cc)])
                        me[("gb", cc)] = inc(v.tensor_copy(
                            out=RBF[:, 4096 + 512 * cc:4608 + 512 * cc], in_=M1[:, :]), "VE")
                    VE(f)
                for mt in range(8):
                    for cc in range(4):
                        c0 = cc * 512
                        par = (mt * 4 + cc) % 2
                        ps1 = PSA[:, 512 * par:512 * par + 512]
                        ps2 = PSB[:, 512 * par:512 * par + 512]
                        mh = {}
                        def f(t, mt=mt, cc=cc, c0=c0, par=par, ps1=ps1, ps2=ps2, me=me, mh=mh, e=e):
                            if mt == 0 and cc == 0:
                                wt(t, "GD", me["w2"])
                                wt(t, "VE", me[("gb", 3)])
                            if ("hv", par) in me:
                                wt(t, "VE", me[("hv", par)])
                            w1v = Db[:, 0:8192].rearrange("p (k o) -> p k o", o=1024)
                            w2v = Db[:, 8192:16384].rearrange("p (k o) -> p k o", o=1024)
                            for kt in range(NT):
                                t.matmul(out=ps1, lhsT=w1v[:, kt, 128 * mt:128 * mt + 128],
                                         rhs=Bv(kt)[:, c0:c0 + 512],
                                         start=(kt == 0), stop=(kt == NT - 1))
                            for kt in range(NT):
                                i = t.matmul(out=ps2, lhsT=w2v[:, kt, 128 * mt:128 * mt + 128],
                                             rhs=Bv(kt)[:, c0:c0 + 512],
                                             start=(kt == 0), stop=(kt == NT - 1))
                            mh["m"] = inc(i, "TE")
                        TE(f)
                        def f(s, mt=mt, e=e, ps1=ps1, par=par, mh=mh):
                            wt(s, "TE", mh["m"])
                            mh["s"] = inc(s.activation(
                                out=PR[:, 2048 + 512 * par:2560 + 512 * par],
                                in_=ps1, func=AF.Silu,
                                bias=SM_bm1[:, 8 * e + mt:8 * e + mt + 1]), "SE")
                        SE(f)
                        def f(v, mt=mt, cc=cc, c0=c0, e=e, ps2=ps2, par=par, me=me, mh=mh):
                            wt(v, "SE", mh["s"])
                            vdrain(v, v.scalar_tensor_tensor(
                                out=PR[:, 3072 + 512 * par:3584 + 512 * par], in0=ps2,
                                scalar=SM_bm2[:, 8 * e + mt:8 * e + mt + 1],
                                in1=PR[:, 2048 + 512 * par:2560 + 512 * par],
                                op0=ALU.add, op1=ALU.mult))
                            me[("hv", par)] = inc(v.tensor_tensor(
                                out=Cv(mt)[:, c0:c0 + 512],
                                in0=PR[:, 3072 + 512 * par:3584 + 512 * par],
                                in1=RBF[:, 4096 + 512 * cc:4608 + 512 * cc], op=ALU.mult), "VE")
                        VE(f)
                ve_h = cnt["VE"]
                if dump == "moe_h" and e == 0:
                    maybe_dump("moe_h", "C")
                def f(g, e=e, me=me, ve_h=ve_h):
                    g.wait_ge(sems["VE"], ve_h)
                    me["wo"] = inc(g.dma_start(
                        out=Db[:, 0:8192].rearrange("p (k o) -> p k o", o=1024),
                        in_=wd["w_mo"][e].rearrange("k p o -> p k o")), "GD", 16)
                GP(f)
                for mt in range(8):
                    for cc in range(4):
                        c0 = cc * 512
                        par = (mt * 4 + cc) % 2
                        ps = M1 if par == 0 else M2
                        mh = {}
                        def f(t, mt=mt, cc=cc, c0=c0, ps=ps, me=me, mh=mh, e=e, par=par):
                            if mt == 0 and cc == 0:
                                wt(t, "GD", me["wo"])
                            if ("ov", par) in me:
                                wt(t, "VE", me[("ov", par)])
                            wov = Db[:, 0:8192].rearrange("p (k o) -> p k o", o=1024)
                            if e == 0:
                                t.matmul(out=ps[:, :], lhsT=SB_bmo[:, 128 * mt:128 * mt + 128],
                                         rhs=PR[0:8, c0:c0 + 512], start=True, stop=False)
                            for kt in range(NT):
                                i = t.matmul(out=ps[:, :], lhsT=wov[:, kt, 128 * mt:128 * mt + 128],
                                             rhs=Cv(kt)[:, c0:c0 + 512],
                                             start=(e != 0 and kt == 0), stop=(kt == NT - 1))
                            mh["m"] = inc(i, "TE")
                        TE(f)
                        def f(v, mt=mt, c0=c0, ps=ps, me=me, mh=mh, par=par):
                            wt(v, "TE", mh["m"])
                            me[("ov", par)] = inc(v.tensor_add(out=Av(mt)[:, c0:c0 + 512],
                                                               in0=Av(mt)[:, c0:c0 + 512],
                                                               in1=ps[:, :]), "VE")
                        VE(f)

        def ph_l1ffn():
            m = {}
            def f(v, m=m):
                m["init"] = inc(v.memset(SM_absw, 0.0), "VE")
            VE(f)
            prev = {}
            for kt in range(NT):
                sl = kt % 2
                def f(sy, kt=kt, sl=sl, m=m, prev=prev):
                    if ("use", sl) in prev:
                        wt(sy, "VE", prev[("use", sl)])
                    m[("w", kt)] = inc(sy.dma_start(out=RCH[:, 1024 * sl:1024 * sl + 1024],
                                                    in_=wd["w_l1"][kt, :, :]), "SY", 16)
                SY(f)
                def f(v, kt=kt, sl=sl, m=m, prev=prev):
                    wt(v, "SY", m[("w", kt)])
                    vdrain(v, v.reduce_max(out=SM_wpart,
                                           in_=RCH[:, 1024 * sl:1024 * sl + 1024], axis=AX,
                                           apply_absolute_value=True))
                    prev[("use", sl)] = vdrain(v, v.tensor_tensor(out=SM_absw, in0=SM_absw,
                                                                  in1=SM_wpart, op=ALU.max))
                VE(f)
            ve_wm = cnt["VE"]
            def f(v, ve_wm=ve_wm):
                i = v.reduce_max(out=SM_absx, in_=A_[:, :], axis=AX,
                                 apply_absolute_value=True)
                vdrain(v, i)
                v.sem_inc(sems["VE"], 1)
            VE(f)
            cnt["VE"] += 1
            ve_xm = cnt["VE"]
            mr = {}
            def f(sy, ve_xm=ve_xm, mr=mr):
                wt(sy, "VE", ve_xm)
                mr["xi"] = inc(sy.dma_start(out=ar_i[0, :], in_=SM_absx[:, 0]), "SY", 16)
                mr["wi"] = inc(sy.dma_start(out=wmx_d[0, :], in_=SM_absw[:, 0]), "SY", 16)
            SY(f)
            def f(g, mr=mr):
                wt(g, "SY", mr["xi"])
                mr["ar"] = inc(g.collective_compute("AllReduce", ALU.max, replica_groups=REPL_ALL,
                                                    ins=[ar_i[:, :]], outs=[ar_o[:, :]]), "GC")
            GP(f)
            def f(sy, mr=mr):
                wt(sy, "GC", mr["ar"])
                mr["xo"] = inc(sy.dma_start(out=SM_arx, in_=ar_o[:, :]), "SY", 16)
                mr["wo"] = inc(sy.dma_start(out=SM_arw, in_=wmx_d[:, :]), "SY", 16)
            SY(f)
            def f(v, mr=mr):
                wt(v, "SY", mr["wo"])
                wt(v, "SY", mr["xo"])
                vdrain(v, v.reduce_max(out=SM_amax, in_=SM_arx, axis=AX))
                mr["mx"] = vdrain(v, v.reduce_max(out=SM_wmax, in_=SM_arw, axis=AX))
            VE(f)
            def f(s, mr=mr):
                wt(s, "VE", mr["mx"])
                s.activation(out=SM_amax, in_=SM_amax, func=AF.Ln)
                s.activation(out=SM_amax, in_=SM_amax, func=AF.Exp, scale=-1.0)
                s.activation(out=SM_wmax, in_=SM_wmax, func=AF.Ln)
                mr["rc"] = inc(s.activation(out=SM_wmax, in_=SM_wmax, func=AF.Exp,
                                            scale=-1.0), "SE")
            SE(f)
            def f(v, mr=mr):
                wt(v, "SE", mr["rc"])
                v.tensor_scalar(out=SM_amax, in0=SM_amax, scalar1=448.0, scalar2=None, op0=ALU.mult)
                vdrain(v, v.tensor_scalar(out=SM_wmax, in0=SM_wmax, scalar1=448.0,
                                          scalar2=None, op0=ALU.mult))
                v.tensor_copy(out=PR[0:1, 0:1], in_=SM_amax)
                mr["sc"] = inc(v.tensor_copy(out=PR[0:1, 1:2], in_=SM_wmax), "VE")
            VE(f)
            def f(t, mr=mr):
                wt(t, "VE", mr["sc"])
                t.matmul(out=M1[:, 0:1], lhsT=SB_ones, rhs=PR[0:1, 0:1], start=True, stop=True)
                mr["bc"] = inc(t.matmul(out=M2[:, 0:1], lhsT=SB_ones, rhs=PR[0:1, 1:2],
                                        start=True, stop=True), "TE")
            TE(f)
            def f(v, mr=mr):
                wt(v, "TE", mr["bc"])
                vdrain(v, v.tensor_copy(out=SM_sxpp, in_=M1[:, 0:1]))
                vdrain(v, v.tensor_copy(out=SM_swpp, in_=M2[:, 0:1]))
                mr["pp"] = vdrain(v, v.tensor_tensor(out=SM_rspp, in0=SM_sxpp, in1=SM_swpp,
                                                     op=ALU.mult))
            VE(f)
            def f(s, mr=mr):
                wt(s, "VE", mr["pp"])
                i = s.activation(out=SM_rspp, in_=SM_rspp, func=AF.Ln)
                mk = inc(i, "SE")
                s.wait_ge(sems["SE"], mk)
                mr["sse"] = inc(s.activation(out=SM_rspp, in_=SM_rspp, func=AF.Exp,
                                             scale=-1.0), "SE")
            SE(f)
            def f(v, mr=mr):
                wt(v, "SE", mr["sse"])
                mr["s"] = inc(v.tensor_copy(out=SM[0:128, 2830:2831], in_=SM_rspp), "VE")
            VE(f)
            if dump == "l1sc":
                def f(sy, mr=mr):
                    wt(sy, "VE", mr["s"])
                    inc(sy.dma_start(out=out_e[0, :, 0:512], in_=SM[:, 2560:3072]), "SY", 16)
                SY(f)
                raise _Done
            def veltq(v, src, dst, sc_ap):
                s1 = RCH[:, 2048:2560]
                s2 = RCH[:, 2560:3072]
                vdrain(v, v.tensor_scalar(out=s1, in0=src, scalar1=sc_ap, scalar2=None,
                                          op0=ALU.mult))
                vdrain(v, v.tensor_scalar(out=s2, in0=s1, scalar1=VELT, scalar2=None,
                                          op0=ALU.mult))
                vdrain(v, v.tensor_sub(out=s1, in0=s2, in1=s1))
                return v.tensor_sub(out=dst, in0=s2, in1=s1)
            prev2 = {("u", 0): prev.get(("use", 0), 0), ("u", 1): prev.get(("use", 1), 0)}
            for kt in range(NT):
                sl = kt % 2
                mw = {}
                def f(sy, kt=kt, sl=sl, mw=mw, prev2=prev2):
                    if ("u", sl) in prev2:
                        wt(sy, "VE", prev2[("u", sl)])
                    mw["w"] = inc(sy.dma_start(out=RCH[:, 1024 * sl:1024 * sl + 1024],
                                               in_=wd["w_l1"][kt, :, :]), "SY", 16)
                SY(f)
                def f(v, kt=kt, sl=sl, mw=mw, mr=mr, prev2=prev2):
                    wt(v, "SY", mw["w"])
                    wt(v, "VE", mr["s"])
                    i = None
                    for hh in range(2):
                        i = veltq(v, RCH[:, 1024 * sl + 512 * hh:1024 * sl + 512 * hh + 512],
                                  Db[:, 1024 * kt + 512 * hh:1024 * kt + 512 * hh + 512],
                                  SM_swpp[:, 0:1])
                    prev2[("u", sl)] = inc(i, "VE")
                VE(f)
            for ft in range(NT):
                def f(v, ft=ft, mr=mr):
                    wt(v, "VE", mr["s"])
                    i = None
                    for hh in range(4):
                        i = veltq(v, Av(ft)[:, 512 * hh:512 * hh + 512],
                                  Bv(ft)[:, 512 * hh:512 * hh + 512], SM_sxpp[:, 0:1])
                    inc(i, "VE")
                VE(f)
            ve_q = cnt["VE"]
            wv_ = Db[:, 0:8192].rearrange("p (k o) -> p k o", o=1024)
            ep = {}
            for mt in range(8):
                for cc in range(4):
                    c0 = cc * 512
                    par = (mt * 4 + cc) % 2
                    ps = M1 if par == 0 else M2
                    mh = {}
                    def f(t, mt=mt, c0=c0, ps=ps, mh=mh, ep=ep, par=par, ve_q=ve_q):
                        if mt == 0 and c0 == 0:
                            wt(t, "VE", ve_q)
                        if ("e", par) in ep:
                            wt(t, "VE", ep[("e", par)])
                        for kt in range(NT):
                            i = t.matmul(out=ps[:, :], lhsT=wv_[:, kt, 128 * mt:128 * mt + 128],
                                         rhs=Bv(kt)[:, c0:c0 + 512],
                                         start=(kt == 0), stop=(kt == NT - 1))
                        mh["m"] = inc(i, "TE")
                    TE(f)
                    def f(v, mt=mt, c0=c0, ps=ps, mh=mh, ep=ep, par=par):
                        wt(v, "TE", mh["m"])
                        vdrain(v, v.tensor_scalar(out=RCH[:, 2048:2560], in0=ps[:, :],
                                                  scalar1=SM_rspp[:, 0:1], scalar2=None,
                                                  op0=ALU.mult))
                        ep[("e", par)] = inc(v.tensor_scalar(
                            out=Av(mt)[:, c0:c0 + 512], in0=RCH[:, 2048:2560],
                            scalar1=SM_bl1[:, mt:mt + 1], scalar2=None, op0=ALU.add), "VE")
                    VE(f)

        class _Done(Exception):
            pass

        def maybe_dump(tag, which="A"):
            if dump != tag:
                return
            snap = dict(cnt)
            if which == "A":
                def f(sy, snap=snap):
                    for s_ in ("VE", "SE", "TE", "GD", "SY"):
                        wt(sy, s_, snap[s_])
                    for ft in range(NT):
                        inc(sy.dma_start(out=out_e[ft, :, :], in_=Av(ft)), "SY", 16)
                SY(f)
            else:
                srcv = Bv if which == "B" else Cv
                def f(g, snap=snap, srcv=srcv):
                    for s_ in ("VE", "SE", "TE", "GD", "SY"):
                        wt(g, s_, snap[s_])
                    for ft in range(NT):
                        inc(g.dma_start(out=out_e[ft, :, :], in_=srcv(ft)), "GD", 16)
                GP(f)
            raise _Done

        seq = [("L0_attn",), ("L0_ffn",), ("L1_attn",), ("L1_ffn",)]
        bnds = [x_in, r0, r1, r0, r1]

        def main_seq():
            ph_boundary(x_in, r0 if nblocks >= 1 else out_e, add_out=False,
                        want_a=True, want_xn=False)
            maybe_dump("b0A")
            for bi in range(min(nblocks, 4)):
                name = seq[bi][0]
                if name == "L0_attn":
                    fence(); ph_rms(g_n1w)
                    maybe_dump("g_xn", "B")
                    attn_layer("g")
                    maybe_dump("b1g")
                    fence(); ph_rms(g_n2w)
                    attn_layer("l")
                    maybe_dump("b1l")
                elif name == "L0_ffn":
                    fence(); ph_rms(f_s1)
                    attn_layer("f")
                    maybe_dump("b2mha")
                    fence(); ph_rms(f_s2)
                    fence(); ph_moe()
                    maybe_dump("b2moe")
                elif name == "L1_attn":
                    attn_layer("a")
                    maybe_dump("b3att")
                else:
                    fence(); ph_l1ffn()
                    maybe_dump("b4lin")
                dst_d = (bnds[bi + 1] if bi + 1 < min(nblocks, 4) else out_e)
                nxt = seq[bi + 1][0] if bi + 1 < min(nblocks, 4) else None
                fence()
                ph_boundary(bnds[bi], dst_d, add_out=True,
                            want_a=(nxt in ("L0_attn", "L0_ffn", "L1_ffn")),
                            want_xn=(nxt == "L1_attn"))
                maybe_dump(f"b{bi+1}A")

        try:
            main_seq()
        except _Done:
            pass
        fin = dict(cnt)
        def f(sy, fin=fin):
            wt(sy, "SY", fin["SY"])
            wt(sy, "VE", fin["VE"])
            wt(sy, "SE", fin["SE"])
            wt(sy, "TE", fin["TE"])
            wt(sy, "GD", fin["GD"])
            wt(sy, "GC", fin["GC"])
        SY(f)
    return nc


def kernel(**inputs):
    in_maps = host_prep(inputs)
    nc = build(4)
    res = run_bass_kernel_spmd(nc, in_maps, core_ids=list(range(8)))
    out = np.zeros((2, S, D), dtype=np.float32)
    for core in range(8):
        b, r = core // 4, core % 4
        toks = np.concatenate([np.arange(2048 * a + 512 * r, 2048 * a + 512 * r + 512)
                               for a in range(NG)])
        out[b, toks, :] = res.results[core]["out"].reshape(D, TOK).T
    return out
